# version 04
# baseline (speedup 1.0000x reference)
import sys
sys.path.insert(0, '/opt/trn_rl_repo')
from contextlib import ExitStack

import numpy as np
import concourse.bass as bass
import concourse.mybir as mybir
from concourse.bass_utils import run_bass_kernel_spmd

F32 = mybir.dt.float32
AF = mybir.ActivationFunctionType
ALU = mybir.AluOpType
AX = mybir.AxisListType

D_MODEL = 2048
D_INNER = 4096
D_STATE = 16
L_FULL = 4096
N_CORES = 8

KB = 16
DB = 32
T1 = 256
NC1 = L_FULL // T1
T2 = 128
NS2 = L_FULL // T2
T4 = 256
NC4 = 2048 // T4
EPS = 1e-5

DEBUG_TAPS = False
PHASES = "all"

ENGS = ("sync", "tensor", "vector", "scalar", "gpsimd")
_ENG_SEM = {"vector": "vv", "scalar": "sv"}


class Plan:

    def __init__(self):
        self.ops = {e: [] for e in ENGS}
        self.cnt = {}
        self._lastwait = {}
        self.sems = {}

    def emit(self, eng, fn):
        self.ops[eng].append(fn)

    def wait(self, eng, sem, val):
        if val is None or val <= 0:
            return
        key = (eng, sem)
        if self._lastwait.get(key, 0) >= val:
            return
        self._lastwait[key] = val
        self.emit(eng, lambda e, s=sem, v=val: e.wait_ge(self.sems[s], v))

    def step(self, eng, sem, build, inc=1):
        own = _ENG_SEM.get(eng)
        if own is not None:
            assert sem == own, f"{eng} step must bump its own sem {own}"
            self.wait(eng, own, self.cnt.get(own, 0))
        self.cnt[sem] = self.cnt.get(sem, 0) + inc
        v = self.cnt[sem]
        def f(e, b=build, s=sem, i=inc):
            b(e).then_inc(self.sems[s], i)
        self.emit(eng, f)
        return v

    def do(self, eng, build):
        own = _ENG_SEM.get(eng)
        if own is not None:
            return self.step(eng, own, build)
        self.emit(eng, lambda e, b=build: b(e))

    def dma(self, eng, sem, out_ap, in_ap):
        return self.step(eng, sem, lambda e, o=out_ap, i=in_ap: e.dma_start(out=o, in_=i), inc=16)


class Ring:

    def __init__(self, plan, tiles):
        self.p = plan
        self.tiles = tiles
        self.n = len(tiles)
        self.head = 0
        self.in_flight = 0
        self.free_at = [0] * self.n
        self.loaded = [0] * self.n
        self.req = []
        self.pending = []

    def _pump(self):
        while self.req and self.in_flight < self.n:
            ap = self.req.pop(0)
            s = self.head % self.n
            self.head += 1
            self.in_flight += 1
            self.p.wait("sync", "tv", self.free_at[s])
            self.loaded[s] = self.p.dma("sync", "w%d" % s, self.tiles[s][:], ap)
            self.pending.append(s)

    def request(self, dram_ap):
        self.req.append(dram_ap)
        self._pump()

    def use(self):
        assert self.pending, "ring underflow: use() without request()"
        s = self.pending.pop(0)
        self.p.wait("tensor", "w%d" % s, self.loaded[s])
        return s, self.tiles[s][:]

    def release(self, s, tv_val):
        self.free_at[s] = tv_val
        self.in_flight -= 1
        self._pump()


def _rearr_feat(dram_ap, p=128):
    return dram_ap.rearrange("(b p) t -> p b t", p=p)


def build_program():
    nc = bass.Bass("TRN2", target_bir_lowering=False,
                   dynamic_dma_scratch_size=16384)
    P = Plan()
    scr_kind = "ExternalOutput" if DEBUG_TAPS else "Internal"

    d = {}
    d["xT"] = nc.dram_tensor("xT", [D_MODEL, L_FULL], F32, kind="ExternalInput").ap()
    d["xoT"] = nc.dram_tensor("xoT", [D_MODEL, 2048], F32, kind="ExternalInput").ap()
    d["w_in_t"] = nc.dram_tensor("w_in_t", [128, 128, 1024], F32, kind="ExternalInput").ap()
    d["w_xp_l"] = nc.dram_tensor("w_xp_l", [128, DB * 160], F32, kind="ExternalInput").ap()
    d["w_dt_t"] = nc.dram_tensor("w_dt_t", [4, 128, 1024], F32, kind="ExternalInput").ap()
    d["w_out_t"] = nc.dram_tensor("w_out_t", [64, 128, 1024], F32, kind="ExternalInput").ap()
    d["w_fc0_t"] = nc.dram_tensor("w_fc0_t", [128, 128, 1024], F32, kind="ExternalInput").ap()
    d["w_fc1_t"] = nc.dram_tensor("w_fc1_t", [128, 128, 1024], F32, kind="ExternalInput").ap()
    d["w_pj0_t"] = nc.dram_tensor("w_pj0_t", [128, 128, 1024], F32, kind="ExternalInput").ap()
    d["w_pj1_t"] = nc.dram_tensor("w_pj1_t", [128, 128, 1024], F32, kind="ExternalInput").ap()
    d["wr_l"] = nc.dram_tensor("wr_l", [128, 32], F32, kind="ExternalInput").ap()
    d["wnorm_l"] = nc.dram_tensor("wnorm_l", [128, KB], F32, kind="ExternalInput").ap()
    d["convw_l"] = nc.dram_tensor("convw_l", [128, DB * 4], F32, kind="ExternalInput").ap()
    d["convb_l"] = nc.dram_tensor("convb_l", [128, DB], F32, kind="ExternalInput").ap()
    d["bdt_l"] = nc.dram_tensor("bdt_l", [128, DB], F32, kind="ExternalInput").ap()
    d["alog_l"] = nc.dram_tensor("alog_l", [128, DB * D_STATE], F32, kind="ExternalInput").ap()
    d["d_l"] = nc.dram_tensor("d_l", [128, DB], F32, kind="ExternalInput").ap()
    d["oh16"] = nc.dram_tensor("oh16", [16, 16 * 128], F32, kind="ExternalInput").ap()
    d["ones_sq"] = nc.dram_tensor("ones_sq", [128, 128], F32, kind="ExternalInput").ap()

    d["xn_scr"] = nc.dram_tensor("xn_scr", [D_MODEL, L_FULL], F32, kind=scr_kind).ap()
    d["xs_scr"] = nc.dram_tensor("xs_scr", [D_INNER, L_FULL], F32, kind=scr_kind).ap()
    d["res_scr"] = nc.dram_tensor("res_scr", [D_INNER, L_FULL], F32, kind=scr_kind).ap()
    d["s_scr"] = nc.dram_tensor("s_scr", [D_MODEL, L_FULL], F32, kind=scr_kind).ap()
    d["g_scr"] = nc.dram_tensor("g_scr", [D_MODEL, 2048], F32, kind=scr_kind).ap()
    d["outT"] = nc.dram_tensor("outT", [D_MODEL, L_FULL], F32, kind="ExternalOutput").ap()

    es = ExitStack()
    sb = lambda name, shape, dt=F32: es.enter_context(nc.sbuf_tensor(name, shape, dt))
    ps = lambda name, shape, dt=F32: es.enter_context(nc.psum_tensor(name, shape, dt))

    with es:
        xt = sb("xt", [128, KB * T1])
        sq = sb("sqb", [128, 1024])
        xzst = sb("xzst", [128, T1 + 6])
        halo = sb("halo", [128, DB * 3])
        convb = sb("convb", [128, T1])
        xsb = sb("xsb", [128, 2 * T1])
        resb = sb("resb", [128, 2 * T1])
        bcb = sb("bcb", [128, 2 * T1])
        ring_tiles = [sb(f"wrg{i}", [128, 1024]) for i in range(4)]
        wxp = sb("wxp", [128, DB * 160])
        xs_s = sb("xs_s", [128, DB * T2])
        res_s = sb("res_s", [128, DB * T2])
        dtr_s = sb("dtr_s", [128, T2])
        bcB_s = sb("bcB_s", [16, T2])
        bcC_s = sb("bcC_s", [16, T2])
        del_s = sb("del_s", [128, DB * T2])
        t2b = sb("t2b", [128, 512])
        a_b = sb("a_b", [128, D_STATE * T2])
        dbu_b = sb("dbu_b", [128, D_STATE * T2])
        h_b = sb("h_b", [128, D_STATE * T2])
        bb_b = sb("bb_b", [128, D_STATE * T2])
        cb_b = sb("cb_b", [128, D_STATE * T2])
        du_b = sb("du_b", [128, T2])
        y_b = sb("y_b", [128, DB * T2])
        hstate = sb("hstate", [128, DB * D_STATE])
        a4 = sb("a4", [128, 4 * T4])
        g4 = sb("g4", [128, KB * T4])
        tmp4 = sb("tmp4", [128, T4])
        wnorm = sb("wnorm", [128, KB])
        convw = sb("convw", [128, DB * 4])
        convbias = sb("convbias", [128, DB])
        bdt = sb("bdt", [128, DB])
        alog = sb("alog", [128, DB * D_STATE])
        dl = sb("dl", [128, DB])
        oh16 = sb("oh16s", [16, 16 * 128])
        onesq = sb("onesq", [128, 128])
        wrl = sb("wrl", [128, 32])

        psP = ps("psP", [128, 1024])
        psS = ps("psS", [128, 512])
        psX = ps("psX", [128, 512])

        for s in ("tv", "vv", "sv",
                  "sq_c", "sq_x", "sq_xo", "sq_st", "sq_xn", "sq_p5",
                  "w0", "w1", "w2", "w3",
                  "gq_xn", "gq_xs", "gq_res", "gq_s", "gq_g", "gq_out"):
            P.sems[s] = es.enter_context(nc.semaphore(s))

        def dma_feat(eng, sem, out3, in3, nb, step=8):
            last = None
            for b0 in range(0, nb, step):
                b1 = min(nb, b0 + step)
                last = P.dma(eng, sem, out3[:, b0:b1], in3[:, b0:b1])
            return last

        ring = Ring(P, ring_tiles)
        free = {"psA": ("vv", 0), "psB": ("vv", 0), "psS": ("vv", 0),
                "psX": ("vv", 0), "sq_buf": ("tv", 0),
                "xsb": ("gq_xs", 0), "resb": ("gq_res", 0),
                "g4": ("gq_g", 0), "tmp4": ("vv", 0), "sq_tmp": ("vv", 0)}

        def twait(res):
            P.wait("tensor", *free[res])

        def mm_group(psl, mms, bump=True):
            for f in mms[:-1]:
                P.do("tensor", f)
            if bump:
                return P.step("tensor", "tv", mms[-1])
            P.do("tensor", mms[-1])
            return None

        for name, tile in (("wnorm_l", wnorm), ("convw_l", convw), ("convb_l", convbias),
                           ("bdt_l", bdt), ("alog_l", alog), ("d_l", dl),
                           ("oh16", oh16), ("ones_sq", onesq),
                           ("wr_l", wrl), ("w_xp_l", wxp)):
            P.dma("sync", "sq_c", tile[:], d[name])
        consts_ld = P.cnt["sq_c"]
        P.wait("scalar", "sq_c", consts_ld)
        P.wait("vector", "sq_c", consts_ld)
        P.wait("tensor", "sq_c", consts_ld)
        P.step("scalar", "sv", lambda e: e.activation(alog[:], alog[:], AF.Exp))
        sv_aneg = P.step("scalar", "sv", lambda e: e.mul(alog[:], alog[:], -1.0))
        P.step("vector", "vv", lambda e: e.memset(hstate[:], 0.0))
        P.step("vector", "vv", lambda e: e.memset(halo[:], 0.0))

        def ln_chunk(xbuf, src_ap, tok0, T, qsem):
            P.wait("sync", "tv", P.cnt.get("tv", 0))
            P.wait("sync", "vv", P.cnt.get("vv", 0))
            ld = dma_feat("sync", qsem, xbuf[:].rearrange("p (b t) -> p b t", b=KB),
                          _rearr_feat(src_ap)[:, :, tok0:tok0 + T], KB)
            P.wait("tensor", qsem, ld)
            twait("psS")
            tv_sx = mm_group(None, [
                (lambda e, b=b, T=T: e.matmul(psS[:, 0:T], onesq[:],
                                              xbuf[:, b * T:(b + 1) * T],
                                              start=(b == 0), stop=(b == KB - 1)))
                for b in range(KB)])
            P.wait("scalar", qsem, ld)
            P.wait("scalar", "vv", free["sq_tmp"][1])
            twait("psA")
            twait("psB")
            kpr = 1024 // T
            tv_sq = None
            for r in range(4):
                P.wait("scalar", "tv", free["sq_buf"][1])
                svq = P.step("scalar", "sv", lambda e, r=r: e.activation(
                    sq[:], xbuf[:, r * 1024:(r + 1) * 1024], AF.Square))
                P.wait("tensor", "sv", svq)
                tv_sq = mm_group(None, [
                    (lambda e, r=r, b=b, T=T: e.matmul(
                        psP[:, r * 256:r * 256 + T], onesq[:],
                        sq[:, b * T:(b + 1) * T],
                        start=(b == 0), stop=(b == kpr - 1)))
                    for b in range(kpr)])
                free["sq_buf"] = ("tv", tv_sq)
            P.wait("vector", "tv", tv_sx)
            P.wait("vector", "tv", tv_sq)
            P.do("vector", lambda e, T=T: e.tensor_copy(bcb[:, 0:T], psP[:, 0:T]))
            P.do("vector", lambda e, T=T: e.tensor_tensor(bcb[:, 0:T], bcb[:, 0:T],
                                                          psP[:, 256:256 + T], ALU.add))
            P.do("vector", lambda e, T=T: e.tensor_tensor(bcb[:, 0:T], bcb[:, 0:T],
                                                          psP[:, 512:512 + T], ALU.add))
            vv_ps = P.step("vector", "vv", lambda e, T=T: e.tensor_tensor(
                bcb[:, 0:T], bcb[:, 0:T], psP[:, 768:768 + T], ALU.add))
            free["psA"] = ("vv", vv_ps)
            free["psB"] = ("vv", vv_ps)
            P.do("vector", lambda e, T=T: e.tensor_scalar(
                bcb[:, 0:T], bcb[:, 0:T], 1.0 / D_MODEL, EPS, ALU.mult, ALU.add))
            vv_mu = P.step("vector", "vv", lambda e, T=T: e.tensor_scalar_mul(
                sq[:, 0:T], psS[:, 0:T], 1.0 / D_MODEL))
            free["psS"] = ("vv", vv_mu)
            P.do("vector", lambda e, T=T: e.tensor_tensor(
                sq[:, 256:256 + T], sq[:, 0:T], sq[:, 0:T], ALU.mult))
            vv_var = P.step("vector", "vv", lambda e, T=T: e.tensor_tensor(
                bcb[:, 0:T], bcb[:, 0:T], sq[:, 256:256 + T], ALU.subtract))
            P.wait("scalar", "vv", vv_var)
            P.do("scalar", lambda e, T=T: e.activation(bcb[:, T:2 * T], bcb[:, 0:T], AF.Ln))
            sv_r = P.step("scalar", "sv", lambda e, T=T: e.activation(
                bcb[:, 0:T], bcb[:, T:2 * T], AF.Exp, scale=-0.5))
            P.wait("vector", "sv", sv_r)
            vv_nmr = P.step("vector", "vv", lambda e, T=T: e.scalar_tensor_tensor(
                bcb[:, T:2 * T], sq[:, 0:T], -1.0, bcb[:, 0:T],
                ALU.mult, ALU.mult))
            free["sq_tmp"] = ("vv", vv_nmr)
            xv = xbuf[:].rearrange("p (b t) -> p b t", b=KB)
            P.do("vector", lambda e, xv=xv, T=T: e.tensor_tensor(
                xv, xv, bcb[:, 0:T].unsqueeze(1).broadcast_to((128, KB, T)), ALU.mult))
            return P.step("vector", "vv", lambda e, xv=xv, T=T: e.tensor_tensor(
                xv, xv, bcb[:, T:2 * T].unsqueeze(1).broadcast_to((128, KB, T)), ALU.add))

        def p1_chunk(ci, ln_only=False):
            tok0 = ci * T1
            vv_xn = ln_chunk(xt, d["xT"], tok0, T1, "sq_x")
            P.wait("gpsimd", "vv", vv_xn)
            gq_xn = dma_feat("gpsimd", "gq_xn",
                             _rearr_feat(d["xn_scr"])[:, :, tok0:tok0 + T1],
                             xt[:].rearrange("p (b t) -> p b t", b=KB), KB)
            twait("psA")
            twait("psB")
            kpr = 1024 // T1
            tv_sq = None
            P.wait("scalar", "vv", vv_xn)
            for r in range(4):
                P.wait("scalar", "tv", free["sq_buf"][1])
                svq = P.step("scalar", "sv", lambda e, r=r: e.activation(
                    sq[:], xt[:, r * 1024:(r + 1) * 1024], AF.Square))
                P.wait("tensor", "sv", svq)
                tv_sq = mm_group(None, [
                    (lambda e, r=r, b=b: e.matmul(
                        psP[:, r * 256:r * 256 + T1], onesq[:],
                        sq[:, b * T1:(b + 1) * T1],
                        start=(b == 0), stop=(b == kpr - 1)))
                    for b in range(kpr)])
                free["sq_buf"] = ("tv", tv_sq)
            P.wait("vector", "tv", tv_sq)
            P.do("vector", lambda e: e.tensor_copy(bcb[:, 0:T1], psP[:, 0:T1]))
            P.do("vector", lambda e: e.tensor_tensor(bcb[:, 0:T1], bcb[:, 0:T1],
                                                     psP[:, 256:256 + T1], ALU.add))
            P.do("vector", lambda e: e.tensor_tensor(bcb[:, 0:T1], bcb[:, 0:T1],
                                                     psP[:, 512:512 + T1], ALU.add))
            vv_ps = P.step("vector", "vv", lambda e: e.tensor_tensor(
                bcb[:, 0:T1], bcb[:, 0:T1], psP[:, 768:768 + T1], ALU.add))
            free["psA"] = ("vv", vv_ps)
            free["psB"] = ("vv", vv_ps)
            vv_m = P.step("vector", "vv", lambda e: e.tensor_scalar(
                bcb[:, 0:T1], bcb[:, 0:T1], 1.0 / D_MODEL, EPS, ALU.mult, ALU.add))
            P.wait("scalar", "vv", vv_m)
            P.do("scalar", lambda e: e.activation(bcb[:, T1:2 * T1], bcb[:, 0:T1], AF.Ln))
            sv_rr = P.step("scalar", "sv", lambda e: e.activation(
                bcb[:, 0:T1], bcb[:, T1:2 * T1], AF.Exp, scale=-0.5))
            P.wait("vector", "sv", sv_rr)
            if ln_only:
                return
            P.wait("vector", "gq_xn", gq_xn)
            for b in range(KB):
                vv_xn2 = P.step("vector", "vv", lambda e, b=b: e.scalar_tensor_tensor(
                    xt[:, b * T1:(b + 1) * T1], xt[:, b * T1:(b + 1) * T1],
                    wnorm[:, b:b + 1], bcb[:, 0:T1], ALU.mult, ALU.mult))

            last_svx = [0]
            ring.request(d["w_in_t"][0])
            ring.request(d["w_in_t"][1])
            for mb in range(64):
                if mb < 63:
                    ring.request(d["w_in_t"][mb * 2 + 2])
                    ring.request(d["w_in_t"][mb * 2 + 3])
                slot_a, ap_a = ring.use()
                slot_b, ap_b = ring.use()
                P.wait("tensor", "vv", vv_xn2)
                psl = psP[:, 0:512] if (mb % 2 == 0) else psP[:, 512:1024]
                pname = "psA" if (mb % 2 == 0) else "psB"
                twait(pname)
                tvm = mm_group(None, [
                    (lambda e, k=k, psl=psl, ap_a=ap_a, ap_b=ap_b: e.matmul(
                        psl[:, 0:T1],
                        (ap_a if k < 8 else ap_b)[:, (k % 8) * 128:(k % 8 + 1) * 128],
                        xt[:, k * T1:(k + 1) * T1],
                        start=(k == 0), stop=(k == 15)))
                    for k in range(16)])
                ring.release(slot_a, tvm)
                ring.release(slot_b, tvm)
                if mb < 32:
                    P.wait("vector", "tv", tvm)
                    vvcp = P.step("vector", "vv", lambda e, psl=psl: e.tensor_copy(
                        xzst[:, 3:3 + T1], psl[:, 0:T1]))
                    free[pname] = ("vv", vvcp)
                    P.do("vector", lambda e, mb=mb: e.tensor_copy(
                        xzst[:, 0:3], halo[:, mb * 3:mb * 3 + 3]))
                    P.do("vector", lambda e, mb=mb: e.tensor_copy(
                        halo[:, mb * 3:mb * 3 + 3], xzst[:, T1:T1 + 3]))
                    P.wait("vector", "sv", last_svx[0])
                    P.do("vector", lambda e, mb=mb: e.tensor_scalar_mul(
                        convb[:], xzst[:, 0:T1], convw[:, mb * 4:mb * 4 + 1]))
                    for k in range(1, 4):
                        P.do("vector", lambda e, mb=mb, k=k: e.scalar_tensor_tensor(
                            convb[:], xzst[:, k:k + T1], convw[:, mb * 4 + k:mb * 4 + k + 1],
                            convb[:], ALU.mult, ALU.add))
                    vvc = P.cnt["vv"]
                    P.wait("scalar", "vv", vvc)
                    P.wait("scalar", "gq_xs", free["xsb"][1])
                    svx = P.step("scalar", "sv", lambda e, mb=mb: e.activation(
                        xsb[:, (mb % 2) * T1:(mb % 2 + 1) * T1], convb[:],
                        AF.Silu, bias=convbias[:, mb:mb + 1]))
                    last_svx[0] = svx
                    if mb % 2 == 1:
                        P.wait("gpsimd", "sv", svx)
                        gqx = P.dma("gpsimd", "gq_xs",
                                    _rearr_feat(d["xs_scr"])[:, mb - 1:mb + 1, tok0:tok0 + T1],
                                    xsb[:].rearrange("p (b t) -> p b t", b=2))
                        free["xsb"] = ("gq_xs", gqx)
                else:
                    P.wait("scalar", "tv", tvm)
                    P.wait("scalar", "gq_res", free["resb"][1])
                    svr = P.step("scalar", "sv", lambda e, mb=mb, psl=psl: e.activation(
                        resb[:, (mb % 2) * T1:(mb % 2 + 1) * T1], psl[:, 0:T1], AF.Silu))
                    free[pname] = ("sv", svr)
                    if mb % 2 == 1:
                        P.wait("gpsimd", "sv", svr)
                        gqr = P.dma("gpsimd", "gq_res",
                                    _rearr_feat(d["res_scr"])[:, mb - 32 - 1:mb - 32 + 1, tok0:tok0 + T1],
                                    resb[:].rearrange("p (b t) -> p b t", b=2))
                        free["resb"] = ("gq_res", gqr)

        strip_free = {"xs_s": ("vv", 0), "res_s": ("vv", 0), "del_s": ("vv", 0)}
        gq_p1_done = {}
        last_gqs = [0]
        last_tvw = [0]

        def p2_super(j):
            tok0 = j * T2
            for fam, val in gq_p1_done.items():
                P.wait("sync", fam, val)
            P.wait("sync", *strip_free["xs_s"])
            dma_feat("sync", "sq_st", xs_s[:].rearrange("p (b t) -> p b t", b=DB),
                     _rearr_feat(d["xs_scr"])[:, :, tok0:tok0 + T2], DB)
            P.wait("sync", *strip_free["res_s"])
            ld_strips = dma_feat("sync", "sq_st", res_s[:].rearrange("p (b t) -> p b t", b=DB),
                                 _rearr_feat(d["res_scr"])[:, :, tok0:tok0 + T2], DB)

            P.wait("tensor", "sq_st", ld_strips)
            twait("psX")
            tv_dtr = mm_group(None, [
                (lambda e, kb=kb: e.matmul(
                    psX[:, 0:T2], wxp[:, kb * 160:kb * 160 + 128],
                    xs_s[:, kb * T2:(kb + 1) * T2],
                    start=(kb == 0), stop=(kb == DB - 1)))
                for kb in range(DB)])
            tv_B = mm_group(None, [
                (lambda e, kb=kb: e.matmul(
                    psX[0:16, 128:128 + T2], wxp[:, kb * 160 + 128:kb * 160 + 144],
                    xs_s[:, kb * T2:(kb + 1) * T2],
                    start=(kb == 0), stop=(kb == DB - 1)))
                for kb in range(DB)])
            tv_C = mm_group(None, [
                (lambda e, kb=kb: e.matmul(
                    psX[0:16, 256:256 + T2], wxp[:, kb * 160 + 144:kb * 160 + 160],
                    xs_s[:, kb * T2:(kb + 1) * T2],
                    start=(kb == 0), stop=(kb == DB - 1)))
                for kb in range(DB)])
            P.wait("vector", "tv", tv_C)
            P.do("vector", lambda e: e.tensor_copy(dtr_s[:], psX[:, 0:T2]))
            P.do("vector", lambda e: e.tensor_copy(bcB_s[:], psX[0:16, 128:128 + T2]))
            vv_bc = P.step("vector", "vv", lambda e: e.tensor_copy(bcC_s[:], psX[0:16, 256:256 + T2]))
            free["psX"] = ("vv", vv_bc)

            P.wait("tensor", "vv", vv_bc)
            for n in range(32):
                dst = (bb_b if n < 16 else cb_b)
                src_bc = bcB_s if n < 16 else bcC_s
                pname = "psA" if (n % 2 == 0) else "psB"
                pso = 512 * (n % 2)
                twait(pname)
                tvn = P.step("tensor", "tv", lambda e, n=n, pso=pso, src_bc=src_bc: e.matmul(
                    psP[:, pso:pso + T2], oh16[:, (n % 16) * 128:(n % 16 + 1) * 128],
                    src_bc[:], start=True, stop=True))
                P.wait("vector", "tv", tvn)
                vvn = P.step("vector", "vv", lambda e, n=n, dst=dst, pso=pso: e.tensor_copy(
                    dst[:, (n % 16) * T2:(n % 16 + 1) * T2], psP[:, pso:pso + T2]))
                free[pname] = ("vv", vvn)

            for t in range(4):
                ring.request(d["w_dt_t"][t])
            wdt_slots = [ring.use() for _ in range(4)]
            tvd_last = 0
            for r in range(8):
                psl = psP[:, 0:512] if (r % 2 == 0) else psP[:, 512:1024]
                pname = "psA" if (r % 2 == 0) else "psB"
                twait(pname)
                sres = wdt_slots[r // 2]
                tvd = mm_group(None, [
                    (lambda e, q=q, psl=psl, sres=sres, r=r: e.matmul(
                        psl[:, q * 128:(q + 1) * 128],
                        sres[1][:, ((4 * r + q) % 8) * 128:((4 * r + q) % 8 + 1) * 128],
                        dtr_s[:], start=(q == 0), stop=(q == 3)))
                    for q in range(4)])
                tvd_last = tvd
                P.wait("vector", "tv", tvd)
                dsl = del_s[:, r * 512:(r + 1) * 512]
                vvz = P.step("vector", "vv", lambda e, r=r, psl=psl, dsl=dsl: e.tensor_tensor(
                    dsl.rearrange("p (b t) -> p b t", b=4),
                    psl[:].rearrange("p (b t) -> p b t", b=4),
                    bdt[:, 4 * r:4 * r + 4].unsqueeze(2).broadcast_to((128, 4, T2)),
                    ALU.add))
                free[pname] = ("vv", vvz)
                P.wait("scalar", "vv", vvz)
                P.do("scalar", lambda e, dsl=dsl: e.activation(t2b[:], dsl, AF.Relu))
                P.do("scalar", lambda e, dsl=dsl: e.activation(dsl, dsl, AF.Abs))
                P.do("scalar", lambda e, dsl=dsl: e.activation(dsl, dsl, AF.Exp, scale=-1.0))
                svd = P.step("scalar", "sv", lambda e, dsl=dsl: e.activation(dsl, dsl, AF.Ln, bias=1.0))
                P.wait("vector", "sv", svd)
                vvd = P.step("vector", "vv", lambda e, dsl=dsl: e.tensor_tensor(
                    dsl, dsl, t2b[:], ALU.add))
                P.wait("scalar", "vv", vvd)
            for s in wdt_slots:
                ring.release(s[0], tvd_last)

            P.wait("vector", "sq_st", ld_strips)
            P.wait("vector", "gq_s", last_gqs[0])
            P.wait("vector", "tv", last_tvw[0])
            prev_scan_vv = [0]
            for i in range(DB):
                dsl = del_s[:, i * T2:(i + 1) * T2]
                P.wait("scalar", "sv", sv_aneg)
                P.wait("scalar", "vv", prev_scan_vv[0])
                for n in range(D_STATE):
                    sv_a = P.step("scalar", "sv", lambda e, i=i, n=n, dsl=dsl: e.activation(
                        a_b[:, n * T2:(n + 1) * T2], dsl, AF.Exp,
                        scale=alog[:, i * D_STATE + n:i * D_STATE + n + 1]))
                P.do("vector", lambda e, i=i, dsl=dsl: e.tensor_tensor(
                    du_b[:], dsl, xs_s[:, i * T2:(i + 1) * T2], ALU.mult))
                P.do("vector", lambda e: e.tensor_tensor(
                    dbu_b[:].rearrange("p (n t) -> p n t", n=D_STATE),
                    du_b[:].unsqueeze(1).broadcast_to((128, D_STATE, T2)),
                    bb_b[:].rearrange("p (n t) -> p n t", n=D_STATE), ALU.mult))
                P.wait("vector", "sv", sv_a)
                for n in range(D_STATE):
                    P.do("vector", lambda e, i=i, n=n: e.tensor_tensor_scan(
                        h_b[:, n * T2:(n + 1) * T2], a_b[:, n * T2:(n + 1) * T2],
                        dbu_b[:, n * T2:(n + 1) * T2],
                        hstate[:, i * D_STATE + n:i * D_STATE + n + 1],
                        ALU.mult, ALU.add))
                prev_scan_vv[0] = P.cnt["vv"]
                P.do("vector", lambda e, i=i: e.tensor_copy(
                    hstate[:, i * D_STATE:(i + 1) * D_STATE], h_b[:, T2 - 1::T2]))
                P.do("vector", lambda e: e.tensor_tensor(h_b[:], h_b[:], cb_b[:], ALU.mult))
                ysl = y_b[:, i * T2:(i + 1) * T2]
                P.do("vector", lambda e, ysl=ysl: e.tensor_reduce(
                    ysl, h_b[:].rearrange("p (n t) -> p t n", n=D_STATE), AX.X, ALU.add))
                P.do("vector", lambda e, i=i, ysl=ysl: e.scalar_tensor_tensor(
                    ysl, xs_s[:, i * T2:(i + 1) * T2], dl[:, i:i + 1], ysl,
                    ALU.mult, ALU.add))
                P.do("vector", lambda e, i=i, ysl=ysl: e.tensor_tensor(
                    ysl, ysl, res_s[:, i * T2:(i + 1) * T2], ALU.mult))
            vv_y = P.cnt["vv"]
            strip_free["xs_s"] = ("vv", vv_y)
            strip_free["res_s"] = ("vv", vv_y)

            P.wait("sync", "vv", vv_y)
            ld_xn = dma_feat("sync", "sq_xn",
                             del_s[:, 0:2048].rearrange("p (b t) -> p b t", b=KB),
                             _rearr_feat(d["xn_scr"])[:, :, tok0:tok0 + T2], KB)

            P.wait("tensor", "vv", vv_y)
            P.wait("vector", "sq_xn", ld_xn)
            for m in range(16):
                ring.request(d["w_out_t"][4 * m + 0])
                ring.request(d["w_out_t"][4 * m + 1])
                ring.request(d["w_out_t"][4 * m + 2])
                ring.request(d["w_out_t"][4 * m + 3])
                tiles = [ring.use() for _ in range(4)]
                psl = psP[:, 0:512] if (m % 2 == 0) else psP[:, 512:1024]
                pname = "psA" if (m % 2 == 0) else "psB"
                twait(pname)
                tvw = mm_group(None, [
                    (lambda e, kb=kb, psl=psl, tiles=tiles: e.matmul(
                        psl[:, 0:T2], tiles[kb // 8][1][:, (kb % 8) * 128:(kb % 8 + 1) * 128],
                        y_b[:, kb * T2:(kb + 1) * T2],
                        start=(kb == 0), stop=(kb == DB - 1)))
                    for kb in range(DB)])
                for t in tiles:
                    ring.release(t[0], tvw)
                last_tvw[0] = tvw
                P.wait("vector", "tv", tvw)
                vvs = P.step("vector", "vv", lambda e, m=m, psl=psl: e.tensor_tensor(
                    h_b[:, m * T2:(m + 1) * T2], psl[:, 0:T2],
                    del_s[:, m * T2:(m + 1) * T2], ALU.add))
                free[pname] = ("vv", vvs)
            strip_free["del_s"] = ("vv", P.cnt["vv"])
            P.wait("gpsimd", "vv", P.cnt["vv"])
            gqs = dma_feat("gpsimd", "gq_s", _rearr_feat(d["s_scr"])[:, :, tok0:tok0 + T2],
                           h_b[:].rearrange("p (b t) -> p b t", b=KB), KB)
            last_gqs[0] = gqs
            return gqs

        def p4_chunk(fci):
            tok0 = fci * T4
            xnf = xt
            vv_xn = ln_chunk(xnf, d["xoT"], tok0, T4, "sq_xo")
            for j in (0, 1):
                twait("psA")
                twait("psB")
                tv_lj = None
                for r in range(4):
                    if r >= 1:
                        P.wait("vector", "tv", tv_lj)
                    for q in range(4):
                        kb = 4 * r + q
                        vvz = P.step("vector", "vv", lambda e, kb=kb, q=q, j=j: e.tensor_scalar_mul(
                            sq[:, q * T4:(q + 1) * T4],
                            xnf[:, kb * T4:(kb + 1) * T4],
                            wrl[:, kb * 2 + j:kb * 2 + j + 1]))
                    P.wait("tensor", "vv", vvz)
                    tv_lj = mm_group(None, [
                        (lambda e, r=r, q=q: e.matmul(
                            psP[:, r * 256:r * 256 + T4], onesq[:],
                            sq[:, q * T4:(q + 1) * T4],
                            start=(q == 0), stop=(q == 3)))
                        for q in range(4)])
                P.wait("vector", "tv", tv_lj)
                lsl = bcb[:, j * T4:(j + 1) * T4]
                P.do("vector", lambda e, lsl=lsl: e.tensor_copy(lsl, psP[:, 0:T4]))
                P.do("vector", lambda e, lsl=lsl: e.tensor_tensor(lsl, lsl, psP[:, 256:256 + T4], ALU.add))
                P.do("vector", lambda e, lsl=lsl: e.tensor_tensor(lsl, lsl, psP[:, 512:512 + T4], ALU.add))
                vv_l = P.step("vector", "vv", lambda e, lsl=lsl: e.tensor_tensor(
                    lsl, lsl, psP[:, 768:768 + T4], ALU.add))
                free["psA"] = ("vv", vv_l)
                free["psB"] = ("vv", vv_l)
            P.do("vector", lambda e: e.tensor_tensor(
                sq[:, 0:T4], bcb[:, T4:2 * T4], bcb[:, 0:T4], ALU.subtract))
            for jexp, (scale, op, woff) in enumerate(
                    ((1.0, ALU.is_le, 0), (-1.0, ALU.is_gt, 1))):
                vvd = P.cnt["vv"]
                P.wait("scalar", "vv", vvd)
                svp = P.step("scalar", "sv", lambda e, scale=scale: e.activation(
                    sq[:, 256:256 + T4], sq[:, 0:T4], AF.Exp, scale=scale))
                P.wait("vector", "sv", svp)
                P.do("vector", lambda e: e.tensor_scalar_add(
                    sq[:, 256:256 + T4], sq[:, 256:256 + T4], 1.0))
                P.do("vector", lambda e: e.reciprocal(
                    sq[:, 256:256 + T4], sq[:, 256:256 + T4]))
                P.do("vector", lambda e, op=op: e.tensor_single_scalar(
                    sq[:, 512:512 + T4], sq[:, 0:T4], 0.0, op))
                vvw = P.step("vector", "vv", lambda e, woff=woff: e.tensor_tensor(
                    bcb[:, woff * T4:(woff + 1) * T4], sq[:, 256:256 + T4],
                    sq[:, 512:512 + T4], ALU.mult))
            free["sq_tmp"] = ("vv", vvw)
            P.wait("vector", "gq_g", free["g4"][1])
            vv_g0 = P.step("vector", "vv", lambda e: e.memset(g4[:], 0.0))

            wfc = (d["w_fc0_t"], d["w_fc1_t"])
            wpj = (d["w_pj0_t"], d["w_pj1_t"])
            proj_tv_prev = [0]
            for e_i in range(2):
                for hq in range(16):
                    for q in range(4):
                        mb = hq * 4 + q
                        ring.request(wfc[e_i][2 * mb])
                        ring.request(wfc[e_i][2 * mb + 1])
                    vv_a4 = 0
                    for q in range(4):
                        mb = hq * 4 + q
                        sa, apa = ring.use()
                        sb_, apb = ring.use()
                        psl = psP[:, 0:512] if (q % 2 == 0) else psP[:, 512:1024]
                        pname = "psA" if (q % 2 == 0) else "psB"
                        twait(pname)
                        tvf = mm_group(None, [
                            (lambda e, k=k, psl=psl, apa=apa, apb=apb: e.matmul(
                                psl[:, 0:T4],
                                (apa if k < 8 else apb)[:, (k % 8) * 128:(k % 8 + 1) * 128],
                                xnf[:, k * T4:(k + 1) * T4],
                                start=(k == 0), stop=(k == 15)))
                            for k in range(16)])
                        ring.release(sa, tvf)
                        ring.release(sb_, tvf)
                        P.wait("scalar", "tv", tvf)
                        P.wait("scalar", "vv", free["tmp4"][1])
                        sv_rl = P.step("scalar", "sv", lambda e, psl=psl: e.activation(
                            tmp4[:], psl[:, 0:T4], AF.Relu, scale=0.5))
                        P.wait("vector", "sv", sv_rl)
                        P.wait("vector", "tv", tvf)
                        vv_pre = P.step("vector", "vv", lambda e, psl=psl: e.scalar_tensor_tensor(
                            tmp4[:], psl[:, 0:T4], 0.5, tmp4[:], ALU.mult, ALU.add))
                        free[pname] = ("vv", vv_pre)
                        P.wait("scalar", "vv", vv_pre)
                        svq = P.step("scalar", "sv", lambda e: e.activation(
                            tmp4[:], tmp4[:], AF.Square))
                        P.wait("vector", "sv", svq)
                        P.wait("vector", "vv", vv_g0)
                        P.wait("vector", "tv", proj_tv_prev[0])
                        vv_a4 = P.step("vector", "vv", lambda e, q=q, e_i=e_i: e.tensor_tensor(
                            a4[:, q * T4:(q + 1) * T4], tmp4[:],
                            bcb[:, e_i * T4:(e_i + 1) * T4], ALU.mult))
                        free["tmp4"] = ("vv", vv_a4)
                    for half in range(2):
                        for kq in range(4):
                            ring.request(wpj[e_i][2 * (hq * 4 + kq) + half])
                        tiles = [ring.use() for _ in range(4)]
                        P.wait("tensor", "vv", vv_a4)
                        tvp_last = 0
                        for m8 in range(8):
                            mg = half * 8 + m8
                            psl = psP[:, 0:512] if (mg % 2 == 0) else psP[:, 512:1024]
                            pname = "psA" if (mg % 2 == 0) else "psB"
                            twait(pname)
                            tvp = mm_group(None, [
                                (lambda e, kq=kq, m8=m8, psl=psl, tiles=tiles: e.matmul(
                                    psl[:, 0:T4], tiles[kq][1][:, m8 * 128:(m8 + 1) * 128],
                                    a4[:, kq * T4:(kq + 1) * T4],
                                    start=(kq == 0), stop=(kq == 3)))
                                for kq in range(4)])
                            tvp_last = tvp
                            P.wait("vector", "tv", tvp)
                            vvg = P.step("vector", "vv", lambda e, mg=mg, psl=psl: e.tensor_tensor(
                                g4[:, mg * T4:(mg + 1) * T4], g4[:, mg * T4:(mg + 1) * T4],
                                psl[:, 0:T4], ALU.add))
                            free[pname] = ("vv", vvg)
                        for t in tiles:
                            ring.release(t[0], tvp_last)
                    proj_tv_prev[0] = P.cnt["tv"]
            P.wait("scalar", "vv", P.cnt["vv"])
            svg = P.step("scalar", "sv", lambda e: e.activation(g4[:], g4[:], AF.Silu))
            P.wait("gpsimd", "sv", svg)
            gqg = dma_feat("gpsimd", "gq_g", _rearr_feat(d["g_scr"])[:, :, tok0:tok0 + T4],
                           g4[:].rearrange("p (b t) -> p b t", b=KB), KB)
            free["g4"] = ("gq_g", gqg)
            return gqg

        def p5_combine(j, g_need, s_need):
            tok0 = j * T2
            g0 = (j * T2) % 2048
            xc = xs_s[:, 0:KB * T2]
            sc = res_s[:, 0:KB * T2]
            gc = del_s[:, 0:KB * T2]
            P.wait("sync", "gq_g", g_need)
            P.wait("sync", "gq_s", s_need)
            P.wait("sync", "vv", p5_vfree[0])
            P.wait("sync", "gq_out", p5_gfree[0])
            dma_feat("sync", "sq_p5", xc.rearrange("p (b t) -> p b t", b=KB),
                     _rearr_feat(d["xT"])[:, :, tok0:tok0 + T2], KB)
            dma_feat("sync", "sq_p5", sc.rearrange("p (b t) -> p b t", b=KB),
                     _rearr_feat(d["s_scr"])[:, :, tok0:tok0 + T2], KB)
            ldg = dma_feat("sync", "sq_p5", gc.rearrange("p (b t) -> p b t", b=KB),
                           _rearr_feat(d["g_scr"])[:, :, g0:g0 + T2], KB)
            P.wait("vector", "sq_p5", ldg)
            P.wait("vector", "gq_out", p5_gfree[0])
            P.do("vector", lambda e, sc=sc, gc=gc: e.tensor_tensor(sc, sc, gc, ALU.mult))
            vvo = P.step("vector", "vv", lambda e, sc=sc, xc=xc: e.tensor_tensor(
                sc, sc, xc, ALU.add))
            p5_vfree[0] = vvo
            P.wait("gpsimd", "vv", vvo)
            gqo = dma_feat("gpsimd", "gq_out", _rearr_feat(d["outT"])[:, :, tok0:tok0 + T2],
                           sc.rearrange("p (b t) -> p b t", b=KB), KB)
            p5_gfree[0] = gqo

        if PHASES == "pre":
            pass
        elif PHASES == "p1ln":
            for ci in range(NC1):
                p1_chunk(ci, ln_only=True)
        elif PHASES.startswith("p1n"):
            for ci in range(int(PHASES[3:])):
                p1_chunk(ci)
        else:
            for ci in range(NC1):
                p1_chunk(ci)
        for fam in ("gq_xn", "gq_xs", "gq_res"):
            gq_p1_done[fam] = P.cnt.get(fam, 0)
        s_done = []
        if PHASES == "pre" or PHASES == "p1ln" or PHASES.startswith("p1n"):
            pass
        elif PHASES in ("p12", "p124", "all"):
            for j in range(NS2):
                s_done.append(p2_super(j))
        g_done = []
        if PHASES in ("p124", "all"):
            for fci in range(NC4):
                g_done.append(p4_chunk(fci))
        if PHASES == "all":
            p5_vfree = [P.cnt["vv"]]
            p5_gfree = [0]
            for j in range(NS2):
                gi = ((j * T2) % 2048) // T4
                p5_combine(j, g_done[gi], s_done[j])
        for fam in ("gq_out", "gq_xn", "gq_xs", "gq_res", "gq_s", "gq_g"):
            tot = P.cnt.get(fam, 0)
            P.do("gpsimd", lambda e, f=fam, t=tot: e.wait_ge(P.sems[f], t))
            P.do("sync", lambda e, f=fam, t=tot: e.wait_ge(P.sems[f], t))

        with nc.Block() as block:
            @block.sync
            def _(eng):
                for f in P.ops["sync"]:
                    f(eng)

            @block.tensor
            def _(eng):
                for f in P.ops["tensor"]:
                    f(eng)

            @block.vector
            def _(eng):
                for f in P.ops["vector"]:
                    f(eng)

            @block.scalar
            def _(eng):
                for f in P.ops["scalar"]:
                    f(eng)

            @block.gpsimd
            def _(eng):
                for f in P.ops["gpsimd"]:
                    f(eng)

    return nc


def _prep_shared(inp):
    f32 = np.float32
    W_in = np.ascontiguousarray(inp["L0_attn_W_in"], dtype=f32)
    W_xp = np.ascontiguousarray(inp["L0_attn_W_xp"], dtype=f32)
    W_dt = np.ascontiguousarray(inp["L0_attn_W_dt"], dtype=f32)
    W_out = np.ascontiguousarray(inp["L0_attn_W_out"], dtype=f32)
    w_fc = np.ascontiguousarray(inp["L0_ffn_w_fc"], dtype=f32)
    w_pj = np.ascontiguousarray(inp["L0_ffn_w_proj"], dtype=f32)
    w_router = np.ascontiguousarray(inp["L0_ffn_w_router"], dtype=f32)

    def ktiles(w):
        Kdim, Mdim = w.shape
        kb, mb = Kdim // 128, Mdim // 128
        t = w.reshape(kb, 128, mb, 128).transpose(2, 1, 0, 3)
        t = t.reshape(mb, 128, kb // 8, 8 * 128).transpose(0, 2, 1, 3)
        return np.ascontiguousarray(t.reshape(mb * (kb // 8), 128, 1024))

    def krows(w):
        Kdim, Mdim = w.shape
        kb = Kdim // 128
        t = w.reshape(kb, 128, Mdim // 1024, 1024).transpose(0, 2, 1, 3)
        return np.ascontiguousarray(t.reshape(kb * (Mdim // 1024), 128, 1024))

    sh = {}
    sh["w_in_t"] = ktiles(W_in)
    sh["w_xp_l"] = np.ascontiguousarray(
        W_xp.reshape(DB, 128, 160).transpose(1, 0, 2).reshape(128, DB * 160))
    sh["w_dt_t"] = np.ascontiguousarray(
        W_dt.reshape(128, 4, 1024).transpose(1, 0, 2))
    sh["w_out_t"] = ktiles(W_out)
    sh["w_fc0_t"] = ktiles(np.ascontiguousarray(w_fc[0].T))
    sh["w_fc1_t"] = ktiles(np.ascontiguousarray(w_fc[1].T))
    sh["w_pj0_t"] = krows(np.ascontiguousarray(w_pj[0].T))
    sh["w_pj1_t"] = krows(np.ascontiguousarray(w_pj[1].T))
    sh["wr_l"] = np.ascontiguousarray(
        w_router.T.reshape(KB, 128, 2).transpose(1, 0, 2).reshape(128, 32))
    sh["wnorm_l"] = np.ascontiguousarray(
        np.asarray(inp["L0_attn_w_norm"], dtype=f32).reshape(KB, 128).T)
    sh["convw_l"] = np.ascontiguousarray(
        np.asarray(inp["L0_attn_conv_w"], dtype=f32).reshape(4, DB, 128)
        .transpose(2, 1, 0).reshape(128, DB * 4))
    sh["convb_l"] = np.ascontiguousarray(
        np.asarray(inp["L0_attn_conv_b"], dtype=f32).reshape(DB, 128).T)
    sh["bdt_l"] = np.ascontiguousarray(
        np.asarray(inp["L0_attn_b_dt"], dtype=f32).reshape(DB, 128).T)
    sh["alog_l"] = np.ascontiguousarray(
        np.asarray(inp["L0_attn_A_log"], dtype=f32).reshape(DB, 128, D_STATE)
        .transpose(1, 0, 2).reshape(128, DB * D_STATE))
    sh["d_l"] = np.ascontiguousarray(
        np.asarray(inp["L0_attn_D"], dtype=f32).reshape(DB, 128).T)
    oh = np.zeros((16, 16 * 128), f32)
    for n in range(16):
        oh[n, n * 128:(n + 1) * 128] = 1.0
    sh["oh16"] = oh
    sh["ones_sq"] = np.ones((128, 128), f32)
    return sh


def kernel(**inputs):
    x = np.ascontiguousarray(np.asarray(inputs["L0_attn_x"], dtype=np.float32))
    B = x.shape[0]
    sh = _prep_shared(inputs)
    nc = build_program()
    in_maps = []
    for c in range(N_CORES):
        b, o = c // 2, c % 2
        m = dict(sh)
        m["xT"] = np.ascontiguousarray(x[b].T)
        m["xoT"] = np.ascontiguousarray(x[b, o * 2048:(o + 1) * 2048].T)
        in_maps.append(m)
    res = run_bass_kernel_spmd(nc, in_maps, core_ids=list(range(N_CORES)))
    out = np.empty((B, L_FULL, D_MODEL), np.float32)
    for c in range(N_CORES):
        b, o = c // 2, c % 2
        outT = np.asarray(res.results[c]["outT"])
        out[b, o * 2048:(o + 1) * 2048, :] = outT[:, o * 2048:(o + 1) * 2048].T
    return out


# version 05 (final)
# speedup vs baseline: 1.0324x; 1.0324x over previous
import sys
sys.path.insert(0, '/opt/trn_rl_repo')
from contextlib import ExitStack

import numpy as np
import concourse.bass as bass
import concourse.mybir as mybir
from concourse.bass_utils import run_bass_kernel_spmd

F32 = mybir.dt.float32
AF = mybir.ActivationFunctionType
ALU = mybir.AluOpType
AX = mybir.AxisListType

D_MODEL = 2048
D_INNER = 4096
D_STATE = 16
L_FULL = 4096
N_CORES = 8

KB = 16
DB = 32
T1 = 256
NC1 = L_FULL // T1
T2 = 128
NS2 = L_FULL // T2
T4 = 256
NC4 = 2048 // T4
EPS = 1e-5

DEBUG_TAPS = False
PHASES = "all"

ENGS = ("sync", "tensor", "vector", "scalar", "gpsimd")
_ENG_SEM = {"vector": "vv", "scalar": "sv"}


class Plan:

    def __init__(self):
        self.ops = {e: [] for e in ENGS}
        self.cnt = {}
        self._lastwait = {}
        self.sems = {}

    def emit(self, eng, fn):
        self.ops[eng].append(fn)

    def wait(self, eng, sem, val):
        if val is None or val <= 0:
            return
        key = (eng, sem)
        if self._lastwait.get(key, 0) >= val:
            return
        self._lastwait[key] = val
        self.emit(eng, lambda e, s=sem, v=val: e.wait_ge(self.sems[s], v))

    def step(self, eng, sem, build, inc=1):
        own = _ENG_SEM.get(eng)
        if own is not None:
            assert sem == own, f"{eng} step must bump its own sem {own}"
            self.wait(eng, own, self.cnt.get(own, 0))
        self.cnt[sem] = self.cnt.get(sem, 0) + inc
        v = self.cnt[sem]
        def f(e, b=build, s=sem, i=inc):
            b(e).then_inc(self.sems[s], i)
        self.emit(eng, f)
        return v

    def do(self, eng, build):
        own = _ENG_SEM.get(eng)
        if own is not None:
            return self.step(eng, own, build)
        self.emit(eng, lambda e, b=build: b(e))

    def dma(self, eng, sem, out_ap, in_ap):
        own = _ENG_SEM.get(eng)
        if own is not None:
            self.wait(eng, own, self.cnt.get(own, 0))
        self.cnt[sem] = self.cnt.get(sem, 0) + 16
        v = self.cnt[sem]
        def f(e, o=out_ap, i=in_ap, s=sem):
            e.dma_start(out=o, in_=i).then_inc(self.sems[s], 16)
        self.emit(eng, f)
        return v


class Ring:

    def __init__(self, plan, tiles):
        self.p = plan
        self.tiles = tiles
        self.n = len(tiles)
        self.head = 0
        self.in_flight = 0
        self.free_at = [0] * self.n
        self.loaded = [0] * self.n
        self.req = []
        self.pending = []

    def _pump(self):
        while self.req and self.in_flight < self.n:
            ap = self.req.pop(0)
            s = self.head % self.n
            self.head += 1
            self.in_flight += 1
            self.p.wait("sync", "tv", self.free_at[s])
            self.loaded[s] = self.p.dma("sync", "w%d" % s, self.tiles[s][:], ap)
            self.pending.append(s)

    def request(self, dram_ap):
        self.req.append(dram_ap)
        self._pump()

    def use(self):
        assert self.pending, "ring underflow: use() without request()"
        s = self.pending.pop(0)
        self.p.wait("tensor", "w%d" % s, self.loaded[s])
        return s, self.tiles[s][:]

    def release(self, s, tv_val):
        self.free_at[s] = tv_val
        self.in_flight -= 1
        self._pump()


def _rearr_feat(dram_ap, p=128):
    return dram_ap.rearrange("(b p) t -> p b t", p=p)


def build_program():
    nc = bass.Bass("TRN2", target_bir_lowering=False,
                   dynamic_dma_scratch_size=2048)
    P = Plan()
    scr_kind = "ExternalOutput" if DEBUG_TAPS else "Internal"

    d = {}
    d["xT"] = nc.dram_tensor("xT", [D_MODEL, L_FULL], F32, kind="ExternalInput").ap()
    d["xoT"] = nc.dram_tensor("xoT", [D_MODEL, 2048], F32, kind="ExternalInput").ap()
    d["w_in_t"] = nc.dram_tensor("w_in_t", [128, 128, 1024], F32, kind="ExternalInput").ap()
    d["w_xp_l"] = nc.dram_tensor("w_xp_l", [128, DB * 160], F32, kind="ExternalInput").ap()
    d["w_dt_t"] = nc.dram_tensor("w_dt_t", [4, 128, 1024], F32, kind="ExternalInput").ap()
    d["w_out_t"] = nc.dram_tensor("w_out_t", [64, 128, 1024], F32, kind="ExternalInput").ap()
    d["w_fc0_t"] = nc.dram_tensor("w_fc0_t", [128, 128, 1024], F32, kind="ExternalInput").ap()
    d["w_fc1_t"] = nc.dram_tensor("w_fc1_t", [128, 128, 1024], F32, kind="ExternalInput").ap()
    d["w_pj0_t"] = nc.dram_tensor("w_pj0_t", [128, 128, 1024], F32, kind="ExternalInput").ap()
    d["w_pj1_t"] = nc.dram_tensor("w_pj1_t", [128, 128, 1024], F32, kind="ExternalInput").ap()
    d["wr_l"] = nc.dram_tensor("wr_l", [128, 32], F32, kind="ExternalInput").ap()
    d["wnorm_l"] = nc.dram_tensor("wnorm_l", [128, KB], F32, kind="ExternalInput").ap()
    d["convw_l"] = nc.dram_tensor("convw_l", [128, DB * 4], F32, kind="ExternalInput").ap()
    d["convb_l"] = nc.dram_tensor("convb_l", [128, DB], F32, kind="ExternalInput").ap()
    d["bdt_l"] = nc.dram_tensor("bdt_l", [128, DB], F32, kind="ExternalInput").ap()
    d["alog_l"] = nc.dram_tensor("alog_l", [128, DB * D_STATE], F32, kind="ExternalInput").ap()
    d["d_l"] = nc.dram_tensor("d_l", [128, DB], F32, kind="ExternalInput").ap()
    d["oh16"] = nc.dram_tensor("oh16", [16, 16 * 128], F32, kind="ExternalInput").ap()
    d["ones_sq"] = nc.dram_tensor("ones_sq", [128, 128], F32, kind="ExternalInput").ap()

    d["xn_scr"] = nc.dram_tensor("xn_scr", [D_MODEL, L_FULL], F32, kind=scr_kind).ap()
    d["xs_scr"] = nc.dram_tensor("xs_scr", [D_INNER, L_FULL], F32, kind=scr_kind).ap()
    d["res_scr"] = nc.dram_tensor("res_scr", [D_INNER, L_FULL], F32, kind=scr_kind).ap()
    d["s_scr"] = nc.dram_tensor("s_scr", [D_MODEL, L_FULL], F32, kind=scr_kind).ap()
    d["g_scr"] = nc.dram_tensor("g_scr", [D_MODEL, 2048], F32, kind=scr_kind).ap()
    d["outT"] = nc.dram_tensor("outT", [D_MODEL, L_FULL], F32, kind="ExternalOutput").ap()

    es = ExitStack()
    sb = lambda name, shape, dt=F32: es.enter_context(nc.sbuf_tensor(name, shape, dt))
    ps = lambda name, shape, dt=F32: es.enter_context(nc.psum_tensor(name, shape, dt))

    with es:
        xt = sb("xt", [128, KB * T1])
        sq = sb("sqb", [128, 1024])
        xzst = sb("xzst", [128, T1 + 6])
        halo = sb("halo", [128, DB * 3])
        convb = sb("convb", [128, T1])
        xsb = sb("xsb", [128, 2 * T1])
        resb = sb("resb", [128, 2 * T1])
        bcb = sb("bcb", [128, 2 * T1])
        ring_tiles = [sb(f"wrg{i}", [128, 1024]) for i in range(6)]
        wxp = sb("wxp", [128, DB * 160])
        xs_s = sb("xs_s", [128, DB * T2])
        res_s = sb("res_s", [128, DB * T2])
        dtr_s = sb("dtr_s", [128, T2])
        bcB_s = sb("bcB_s", [16, T2])
        bcC_s = sb("bcC_s", [16, T2])
        del_s = sb("del_s", [128, DB * T2])
        t2b = sb("t2b", [128, 512])
        a_b = sb("a_b", [128, D_STATE * T2])
        dbu_b = sb("dbu_b", [128, D_STATE * T2])
        h_b = sb("h_b", [128, D_STATE * T2])
        bb_b = sb("bb_b", [128, D_STATE * T2])
        cb_b = sb("cb_b", [128, D_STATE * T2])
        du_b = sb("du_b", [128, T2])
        y_b = sb("y_b", [128, DB * T2])
        hstate = sb("hstate", [128, DB * D_STATE])
        a4 = sb("a4", [128, 4 * T4])
        g4 = sb("g4", [128, KB * T4])
        tmp4 = sb("tmp4", [128, T4])
        wnorm = sb("wnorm", [128, KB])
        convw = sb("convw", [128, DB * 4])
        convbias = sb("convbias", [128, DB])
        bdt = sb("bdt", [128, DB])
        alog = sb("alog", [128, DB * D_STATE])
        dl = sb("dl", [128, DB])
        oh16 = sb("oh16s", [16, 16 * 128])
        onesq = sb("onesq", [128, 128])
        wrl = sb("wrl", [128, 32])

        psP = ps("psP", [128, 1024])
        psS = ps("psS", [128, 512])
        psX = ps("psX", [128, 512])

        for s in ("tv", "vv", "sv",
                  "sq_c", "sq_x", "sq_xo", "sq_st", "sq_xn", "sq_p5",
                  "w0", "w1", "w2", "w3", "w4", "w5",
                  "gq_xn", "gq_xs", "gq_res", "gq_s", "gq_g", "gq_out"):
            P.sems[s] = es.enter_context(nc.semaphore(s))

        def dma_feat(eng, sem, out3, in3, nb, step=8):
            last = None
            for b0 in range(0, nb, step):
                b1 = min(nb, b0 + step)
                last = P.dma(eng, sem, out3[:, b0:b1], in3[:, b0:b1])
            return last

        ring = Ring(P, ring_tiles)
        free = {"psA": ("vv", 0), "psB": ("vv", 0), "psS": ("vv", 0),
                "psX": ("vv", 0), "sq_buf": ("tv", 0),
                "xsb": ("gq_xs", 0), "resb": ("gq_res", 0),
                "g4": ("gq_g", 0), "tmp4": ("vv", 0), "sq_tmp": ("vv", 0)}

        def twait(res):
            P.wait("tensor", *free[res])

        def mm_group(psl, mms, bump=True):
            for f in mms[:-1]:
                P.do("tensor", f)
            if bump:
                return P.step("tensor", "tv", mms[-1])
            P.do("tensor", mms[-1])
            return None

        for name, tile in (("wnorm_l", wnorm), ("convw_l", convw), ("convb_l", convbias),
                           ("bdt_l", bdt), ("alog_l", alog), ("d_l", dl),
                           ("oh16", oh16), ("ones_sq", onesq),
                           ("wr_l", wrl), ("w_xp_l", wxp)):
            P.dma("sync", "sq_c", tile[:], d[name])
        consts_ld = P.cnt["sq_c"]
        P.wait("scalar", "sq_c", consts_ld)
        P.wait("vector", "sq_c", consts_ld)
        P.wait("tensor", "sq_c", consts_ld)
        P.step("scalar", "sv", lambda e: e.activation(alog[:], alog[:], AF.Exp))
        sv_aneg = P.step("scalar", "sv", lambda e: e.mul(alog[:], alog[:], -1.0))
        P.step("vector", "vv", lambda e: e.memset(hstate[:], 0.0))
        P.step("vector", "vv", lambda e: e.memset(halo[:], 0.0))

        def ln_chunk(xbuf, src_ap, tok0, T, qsem):
            P.wait("sync", "tv", P.cnt.get("tv", 0))
            P.wait("sync", "vv", P.cnt.get("vv", 0))
            ld = dma_feat("sync", qsem, xbuf[:].rearrange("p (b t) -> p b t", b=KB),
                          _rearr_feat(src_ap)[:, :, tok0:tok0 + T], KB)
            P.wait("tensor", qsem, ld)
            twait("psS")
            tv_sx = mm_group(None, [
                (lambda e, b=b, T=T: e.matmul(psS[:, 0:T], onesq[:],
                                              xbuf[:, b * T:(b + 1) * T],
                                              start=(b == 0), stop=(b == KB - 1)))
                for b in range(KB)])
            P.wait("scalar", qsem, ld)
            P.wait("scalar", "vv", free["sq_tmp"][1])
            twait("psA")
            twait("psB")
            kpr = 1024 // T
            tv_sq = None
            for r in range(4):
                P.wait("scalar", "tv", free["sq_buf"][1])
                svq = P.step("scalar", "sv", lambda e, r=r: e.activation(
                    sq[:], xbuf[:, r * 1024:(r + 1) * 1024], AF.Square))
                P.wait("tensor", "sv", svq)
                tv_sq = mm_group(None, [
                    (lambda e, r=r, b=b, T=T: e.matmul(
                        psP[:, r * 256:r * 256 + T], onesq[:],
                        sq[:, b * T:(b + 1) * T],
                        start=(b == 0), stop=(b == kpr - 1)))
                    for b in range(kpr)])
                free["sq_buf"] = ("tv", tv_sq)
            P.wait("vector", "tv", tv_sx)
            P.wait("vector", "tv", tv_sq)
            P.do("vector", lambda e, T=T: e.tensor_copy(bcb[:, 0:T], psP[:, 0:T]))
            P.do("vector", lambda e, T=T: e.tensor_tensor(bcb[:, 0:T], bcb[:, 0:T],
                                                          psP[:, 256:256 + T], ALU.add))
            P.do("vector", lambda e, T=T: e.tensor_tensor(bcb[:, 0:T], bcb[:, 0:T],
                                                          psP[:, 512:512 + T], ALU.add))
            vv_ps = P.step("vector", "vv", lambda e, T=T: e.tensor_tensor(
                bcb[:, 0:T], bcb[:, 0:T], psP[:, 768:768 + T], ALU.add))
            free["psA"] = ("vv", vv_ps)
            free["psB"] = ("vv", vv_ps)
            P.do("vector", lambda e, T=T: e.tensor_scalar(
                bcb[:, 0:T], bcb[:, 0:T], 1.0 / D_MODEL, EPS, ALU.mult, ALU.add))
            vv_mu = P.step("vector", "vv", lambda e, T=T: e.tensor_scalar_mul(
                sq[:, 0:T], psS[:, 0:T], 1.0 / D_MODEL))
            free["psS"] = ("vv", vv_mu)
            P.do("vector", lambda e, T=T: e.tensor_tensor(
                sq[:, 256:256 + T], sq[:, 0:T], sq[:, 0:T], ALU.mult))
            vv_var = P.step("vector", "vv", lambda e, T=T: e.tensor_tensor(
                bcb[:, 0:T], bcb[:, 0:T], sq[:, 256:256 + T], ALU.subtract))
            P.wait("scalar", "vv", vv_var)
            P.do("scalar", lambda e, T=T: e.activation(bcb[:, T:2 * T], bcb[:, 0:T], AF.Ln))
            sv_r = P.step("scalar", "sv", lambda e, T=T: e.activation(
                bcb[:, 0:T], bcb[:, T:2 * T], AF.Exp, scale=-0.5))
            P.wait("vector", "sv", sv_r)
            vv_nmr = P.step("vector", "vv", lambda e, T=T: e.scalar_tensor_tensor(
                bcb[:, T:2 * T], sq[:, 0:T], -1.0, bcb[:, 0:T],
                ALU.mult, ALU.mult))
            free["sq_tmp"] = ("vv", vv_nmr)
            xv = xbuf[:].rearrange("p (b t) -> p b t", b=KB)
            P.do("vector", lambda e, xv=xv, T=T: e.tensor_tensor(
                xv, xv, bcb[:, 0:T].unsqueeze(1).broadcast_to((128, KB, T)), ALU.mult))
            return P.step("vector", "vv", lambda e, xv=xv, T=T: e.tensor_tensor(
                xv, xv, bcb[:, T:2 * T].unsqueeze(1).broadcast_to((128, KB, T)), ALU.add))

        def p1_chunk(ci, ln_only=False):
            tok0 = ci * T1
            vv_xn = ln_chunk(xt, d["xT"], tok0, T1, "sq_x")
            P.wait("scalar", "vv", vv_xn)
            gq_xn = dma_feat("scalar", "gq_xn",
                             _rearr_feat(d["xn_scr"])[:, :, tok0:tok0 + T1],
                             xt[:].rearrange("p (b t) -> p b t", b=KB), KB)
            twait("psA")
            twait("psB")
            kpr = 1024 // T1
            tv_sq = None
            P.wait("scalar", "vv", vv_xn)
            for r in range(4):
                P.wait("scalar", "tv", free["sq_buf"][1])
                svq = P.step("scalar", "sv", lambda e, r=r: e.activation(
                    sq[:], xt[:, r * 1024:(r + 1) * 1024], AF.Square))
                P.wait("tensor", "sv", svq)
                tv_sq = mm_group(None, [
                    (lambda e, r=r, b=b: e.matmul(
                        psP[:, r * 256:r * 256 + T1], onesq[:],
                        sq[:, b * T1:(b + 1) * T1],
                        start=(b == 0), stop=(b == kpr - 1)))
                    for b in range(kpr)])
                free["sq_buf"] = ("tv", tv_sq)
            P.wait("vector", "tv", tv_sq)
            P.do("vector", lambda e: e.tensor_copy(bcb[:, 0:T1], psP[:, 0:T1]))
            P.do("vector", lambda e: e.tensor_tensor(bcb[:, 0:T1], bcb[:, 0:T1],
                                                     psP[:, 256:256 + T1], ALU.add))
            P.do("vector", lambda e: e.tensor_tensor(bcb[:, 0:T1], bcb[:, 0:T1],
                                                     psP[:, 512:512 + T1], ALU.add))
            vv_ps = P.step("vector", "vv", lambda e: e.tensor_tensor(
                bcb[:, 0:T1], bcb[:, 0:T1], psP[:, 768:768 + T1], ALU.add))
            free["psA"] = ("vv", vv_ps)
            free["psB"] = ("vv", vv_ps)
            vv_m = P.step("vector", "vv", lambda e: e.tensor_scalar(
                bcb[:, 0:T1], bcb[:, 0:T1], 1.0 / D_MODEL, EPS, ALU.mult, ALU.add))
            P.wait("scalar", "vv", vv_m)
            P.do("scalar", lambda e: e.activation(bcb[:, T1:2 * T1], bcb[:, 0:T1], AF.Ln))
            sv_rr = P.step("scalar", "sv", lambda e: e.activation(
                bcb[:, 0:T1], bcb[:, T1:2 * T1], AF.Exp, scale=-0.5))
            P.wait("vector", "sv", sv_rr)
            if ln_only:
                return
            P.wait("vector", "gq_xn", gq_xn)
            for b in range(KB):
                vv_xn2 = P.step("vector", "vv", lambda e, b=b: e.scalar_tensor_tensor(
                    xt[:, b * T1:(b + 1) * T1], xt[:, b * T1:(b + 1) * T1],
                    wnorm[:, b:b + 1], bcb[:, 0:T1], ALU.mult, ALU.mult))

            last_svx = [0]
            ring.request(d["w_in_t"][0])
            ring.request(d["w_in_t"][1])
            for mb in range(64):
                if mb < 63:
                    ring.request(d["w_in_t"][mb * 2 + 2])
                    ring.request(d["w_in_t"][mb * 2 + 3])
                slot_a, ap_a = ring.use()
                slot_b, ap_b = ring.use()
                P.wait("tensor", "vv", vv_xn2)
                psl = psP[:, 0:512] if (mb % 2 == 0) else psP[:, 512:1024]
                pname = "psA" if (mb % 2 == 0) else "psB"
                twait(pname)
                tvm = mm_group(None, [
                    (lambda e, k=k, psl=psl, ap_a=ap_a, ap_b=ap_b: e.matmul(
                        psl[:, 0:T1],
                        (ap_a if k < 8 else ap_b)[:, (k % 8) * 128:(k % 8 + 1) * 128],
                        xt[:, k * T1:(k + 1) * T1],
                        start=(k == 0), stop=(k == 15)))
                    for k in range(16)])
                ring.release(slot_a, tvm)
                ring.release(slot_b, tvm)
                if mb < 32:
                    P.wait("vector", "tv", tvm)
                    vvcp = P.step("vector", "vv", lambda e, psl=psl: e.tensor_copy(
                        xzst[:, 3:3 + T1], psl[:, 0:T1]))
                    free[pname] = ("vv", vvcp)
                    P.do("vector", lambda e, mb=mb: e.tensor_copy(
                        xzst[:, 0:3], halo[:, mb * 3:mb * 3 + 3]))
                    P.do("vector", lambda e, mb=mb: e.tensor_copy(
                        halo[:, mb * 3:mb * 3 + 3], xzst[:, T1:T1 + 3]))
                    P.wait("vector", "sv", last_svx[0])
                    P.do("vector", lambda e, mb=mb: e.tensor_scalar_mul(
                        convb[:], xzst[:, 0:T1], convw[:, mb * 4:mb * 4 + 1]))
                    for k in range(1, 4):
                        P.do("vector", lambda e, mb=mb, k=k: e.scalar_tensor_tensor(
                            convb[:], xzst[:, k:k + T1], convw[:, mb * 4 + k:mb * 4 + k + 1],
                            convb[:], ALU.mult, ALU.add))
                    vvc = P.cnt["vv"]
                    P.wait("scalar", "vv", vvc)
                    P.wait("scalar", "gq_xs", free["xsb"][1])
                    svx = P.step("scalar", "sv", lambda e, mb=mb: e.activation(
                        xsb[:, (mb % 2) * T1:(mb % 2 + 1) * T1], convb[:],
                        AF.Silu, bias=convbias[:, mb:mb + 1]))
                    last_svx[0] = svx
                    if mb % 2 == 1:
                        P.wait("scalar", "sv", svx)
                        gqx = P.dma("scalar", "gq_xs",
                                    _rearr_feat(d["xs_scr"])[:, mb - 1:mb + 1, tok0:tok0 + T1],
                                    xsb[:].rearrange("p (b t) -> p b t", b=2))
                        free["xsb"] = ("gq_xs", gqx)
                else:
                    P.wait("scalar", "tv", tvm)
                    P.wait("scalar", "gq_res", free["resb"][1])
                    svr = P.step("scalar", "sv", lambda e, mb=mb, psl=psl: e.activation(
                        resb[:, (mb % 2) * T1:(mb % 2 + 1) * T1], psl[:, 0:T1], AF.Silu))
                    free[pname] = ("sv", svr)
                    if mb % 2 == 1:
                        P.wait("scalar", "sv", svr)
                        gqr = P.dma("scalar", "gq_res",
                                    _rearr_feat(d["res_scr"])[:, mb - 32 - 1:mb - 32 + 1, tok0:tok0 + T1],
                                    resb[:].rearrange("p (b t) -> p b t", b=2))
                        free["resb"] = ("gq_res", gqr)

        strip_free = {"xs_s": ("vv", 0), "res_s": ("vv", 0), "del_s": ("vv", 0)}
        gq_p1_done = {}
        last_gqs = [0]
        last_tvw = [0]

        def p2_super(j):
            tok0 = j * T2
            for fam, val in gq_p1_done.items():
                P.wait("sync", fam, val)
            P.wait("sync", *strip_free["xs_s"])
            dma_feat("sync", "sq_st", xs_s[:].rearrange("p (b t) -> p b t", b=DB),
                     _rearr_feat(d["xs_scr"])[:, :, tok0:tok0 + T2], DB)
            P.wait("sync", *strip_free["res_s"])
            ld_strips = dma_feat("sync", "sq_st", res_s[:].rearrange("p (b t) -> p b t", b=DB),
                                 _rearr_feat(d["res_scr"])[:, :, tok0:tok0 + T2], DB)

            P.wait("tensor", "sq_st", ld_strips)
            twait("psX")
            tv_dtr = mm_group(None, [
                (lambda e, kb=kb: e.matmul(
                    psX[:, 0:T2], wxp[:, kb * 160:kb * 160 + 128],
                    xs_s[:, kb * T2:(kb + 1) * T2],
                    start=(kb == 0), stop=(kb == DB - 1)))
                for kb in range(DB)])
            tv_B = mm_group(None, [
                (lambda e, kb=kb: e.matmul(
                    psX[0:16, 128:128 + T2], wxp[:, kb * 160 + 128:kb * 160 + 144],
                    xs_s[:, kb * T2:(kb + 1) * T2],
                    start=(kb == 0), stop=(kb == DB - 1)))
                for kb in range(DB)])
            tv_C = mm_group(None, [
                (lambda e, kb=kb: e.matmul(
                    psX[0:16, 256:256 + T2], wxp[:, kb * 160 + 144:kb * 160 + 160],
                    xs_s[:, kb * T2:(kb + 1) * T2],
                    start=(kb == 0), stop=(kb == DB - 1)))
                for kb in range(DB)])
            P.wait("vector", "tv", tv_C)
            P.do("vector", lambda e: e.tensor_copy(dtr_s[:], psX[:, 0:T2]))
            P.do("vector", lambda e: e.tensor_copy(bcB_s[:], psX[0:16, 128:128 + T2]))
            vv_bc = P.step("vector", "vv", lambda e: e.tensor_copy(bcC_s[:], psX[0:16, 256:256 + T2]))
            free["psX"] = ("vv", vv_bc)

            P.wait("tensor", "vv", vv_bc)
            for n in range(32):
                dst = (bb_b if n < 16 else cb_b)
                src_bc = bcB_s if n < 16 else bcC_s
                pname = "psA" if (n % 2 == 0) else "psB"
                pso = 512 * (n % 2)
                twait(pname)
                tvn = P.step("tensor", "tv", lambda e, n=n, pso=pso, src_bc=src_bc: e.matmul(
                    psP[:, pso:pso + T2], oh16[:, (n % 16) * 128:(n % 16 + 1) * 128],
                    src_bc[:], start=True, stop=True))
                P.wait("vector", "tv", tvn)
                vvn = P.step("vector", "vv", lambda e, n=n, dst=dst, pso=pso: e.tensor_copy(
                    dst[:, (n % 16) * T2:(n % 16 + 1) * T2], psP[:, pso:pso + T2]))
                free[pname] = ("vv", vvn)

            for t in range(4):
                ring.request(d["w_dt_t"][t])
            wdt_slots = [ring.use() for _ in range(4)]
            tvd_last = 0
            for r in range(8):
                psl = psP[:, 0:512] if (r % 2 == 0) else psP[:, 512:1024]
                pname = "psA" if (r % 2 == 0) else "psB"
                twait(pname)
                sres = wdt_slots[r // 2]
                tvd = mm_group(None, [
                    (lambda e, q=q, psl=psl, sres=sres, r=r: e.matmul(
                        psl[:, q * 128:(q + 1) * 128],
                        sres[1][:, ((4 * r + q) % 8) * 128:((4 * r + q) % 8 + 1) * 128],
                        dtr_s[:], start=(q == 0), stop=(q == 3)))
                    for q in range(4)])
                tvd_last = tvd
                P.wait("vector", "tv", tvd)
                dsl = del_s[:, r * 512:(r + 1) * 512]
                vvz = P.step("vector", "vv", lambda e, r=r, psl=psl, dsl=dsl: e.tensor_tensor(
                    dsl.rearrange("p (b t) -> p b t", b=4),
                    psl[:].rearrange("p (b t) -> p b t", b=4),
                    bdt[:, 4 * r:4 * r + 4].unsqueeze(2).broadcast_to((128, 4, T2)),
                    ALU.add))
                free[pname] = ("vv", vvz)
                P.wait("scalar", "vv", vvz)
                P.do("scalar", lambda e, dsl=dsl: e.activation(t2b[:], dsl, AF.Relu))
                P.do("scalar", lambda e, dsl=dsl: e.activation(dsl, dsl, AF.Abs))
                P.do("scalar", lambda e, dsl=dsl: e.activation(dsl, dsl, AF.Exp, scale=-1.0))
                svd = P.step("scalar", "sv", lambda e, dsl=dsl: e.activation(dsl, dsl, AF.Ln, bias=1.0))
                P.wait("vector", "sv", svd)
                vvd = P.step("vector", "vv", lambda e, dsl=dsl: e.tensor_tensor(
                    dsl, dsl, t2b[:], ALU.add))
                P.wait("scalar", "vv", vvd)
            for s in wdt_slots:
                ring.release(s[0], tvd_last)

            P.wait("vector", "sq_st", ld_strips)
            P.wait("vector", "gq_s", last_gqs[0])
            P.wait("vector", "tv", last_tvw[0])
            prev_scan_vv = [0]
            for i in range(DB):
                dsl = del_s[:, i * T2:(i + 1) * T2]
                P.wait("scalar", "sv", sv_aneg)
                P.wait("scalar", "vv", prev_scan_vv[0])
                for n in range(D_STATE):
                    sv_a = P.step("scalar", "sv", lambda e, i=i, n=n, dsl=dsl: e.activation(
                        a_b[:, n * T2:(n + 1) * T2], dsl, AF.Exp,
                        scale=alog[:, i * D_STATE + n:i * D_STATE + n + 1]))
                P.do("vector", lambda e, i=i, dsl=dsl: e.tensor_tensor(
                    du_b[:], dsl, xs_s[:, i * T2:(i + 1) * T2], ALU.mult))
                P.do("vector", lambda e: e.tensor_tensor(
                    dbu_b[:].rearrange("p (n t) -> p n t", n=D_STATE),
                    du_b[:].unsqueeze(1).broadcast_to((128, D_STATE, T2)),
                    bb_b[:].rearrange("p (n t) -> p n t", n=D_STATE), ALU.mult))
                P.wait("vector", "sv", sv_a)
                for n in range(D_STATE):
                    P.do("vector", lambda e, i=i, n=n: e.tensor_tensor_scan(
                        h_b[:, n * T2:(n + 1) * T2], a_b[:, n * T2:(n + 1) * T2],
                        dbu_b[:, n * T2:(n + 1) * T2],
                        hstate[:, i * D_STATE + n:i * D_STATE + n + 1],
                        ALU.mult, ALU.add))
                prev_scan_vv[0] = P.cnt["vv"]
                P.do("vector", lambda e, i=i: e.tensor_copy(
                    hstate[:, i * D_STATE:(i + 1) * D_STATE], h_b[:, T2 - 1::T2]))
                P.do("vector", lambda e: e.tensor_tensor(h_b[:], h_b[:], cb_b[:], ALU.mult))
                ysl = y_b[:, i * T2:(i + 1) * T2]
                P.do("vector", lambda e, ysl=ysl: e.tensor_reduce(
                    ysl, h_b[:].rearrange("p (n t) -> p t n", n=D_STATE), AX.X, ALU.add))
                P.do("vector", lambda e, i=i, ysl=ysl: e.scalar_tensor_tensor(
                    ysl, xs_s[:, i * T2:(i + 1) * T2], dl[:, i:i + 1], ysl,
                    ALU.mult, ALU.add))
                P.do("vector", lambda e, i=i, ysl=ysl: e.tensor_tensor(
                    ysl, ysl, res_s[:, i * T2:(i + 1) * T2], ALU.mult))
            vv_y = P.cnt["vv"]
            strip_free["xs_s"] = ("vv", vv_y)
            strip_free["res_s"] = ("vv", vv_y)

            P.wait("sync", "vv", vv_y)
            ld_xn = dma_feat("sync", "sq_xn",
                             del_s[:, 0:2048].rearrange("p (b t) -> p b t", b=KB),
                             _rearr_feat(d["xn_scr"])[:, :, tok0:tok0 + T2], KB)

            P.wait("tensor", "vv", vv_y)
            P.wait("vector", "sq_xn", ld_xn)
            for m in range(16):
                ring.request(d["w_out_t"][4 * m + 0])
                ring.request(d["w_out_t"][4 * m + 1])
                ring.request(d["w_out_t"][4 * m + 2])
                ring.request(d["w_out_t"][4 * m + 3])
                tiles = [ring.use() for _ in range(4)]
                psl = psP[:, 0:512] if (m % 2 == 0) else psP[:, 512:1024]
                pname = "psA" if (m % 2 == 0) else "psB"
                twait(pname)
                tvw = mm_group(None, [
                    (lambda e, kb=kb, psl=psl, tiles=tiles: e.matmul(
                        psl[:, 0:T2], tiles[kb // 8][1][:, (kb % 8) * 128:(kb % 8 + 1) * 128],
                        y_b[:, kb * T2:(kb + 1) * T2],
                        start=(kb == 0), stop=(kb == DB - 1)))
                    for kb in range(DB)])
                for t in tiles:
                    ring.release(t[0], tvw)
                last_tvw[0] = tvw
                P.wait("vector", "tv", tvw)
                vvs = P.step("vector", "vv", lambda e, m=m, psl=psl: e.tensor_tensor(
                    h_b[:, m * T2:(m + 1) * T2], psl[:, 0:T2],
                    del_s[:, m * T2:(m + 1) * T2], ALU.add))
                free[pname] = ("vv", vvs)
            strip_free["del_s"] = ("vv", P.cnt["vv"])
            P.wait("scalar", "vv", P.cnt["vv"])
            gqs = dma_feat("scalar", "gq_s", _rearr_feat(d["s_scr"])[:, :, tok0:tok0 + T2],
                           h_b[:].rearrange("p (b t) -> p b t", b=KB), KB)
            last_gqs[0] = gqs
            return gqs

        def p4_chunk(fci):
            tok0 = fci * T4
            xnf = xt
            vv_xn = ln_chunk(xnf, d["xoT"], tok0, T4, "sq_xo")
            for j in (0, 1):
                twait("psA")
                twait("psB")
                tv_lj = None
                for r in range(4):
                    if r >= 1:
                        P.wait("vector", "tv", tv_lj)
                    for q in range(4):
                        kb = 4 * r + q
                        vvz = P.step("vector", "vv", lambda e, kb=kb, q=q, j=j: e.tensor_scalar_mul(
                            sq[:, q * T4:(q + 1) * T4],
                            xnf[:, kb * T4:(kb + 1) * T4],
                            wrl[:, kb * 2 + j:kb * 2 + j + 1]))
                    P.wait("tensor", "vv", vvz)
                    tv_lj = mm_group(None, [
                        (lambda e, r=r, q=q: e.matmul(
                            psP[:, r * 256:r * 256 + T4], onesq[:],
                            sq[:, q * T4:(q + 1) * T4],
                            start=(q == 0), stop=(q == 3)))
                        for q in range(4)])
                P.wait("vector", "tv", tv_lj)
                lsl = bcb[:, j * T4:(j + 1) * T4]
                P.do("vector", lambda e, lsl=lsl: e.tensor_copy(lsl, psP[:, 0:T4]))
                P.do("vector", lambda e, lsl=lsl: e.tensor_tensor(lsl, lsl, psP[:, 256:256 + T4], ALU.add))
                P.do("vector", lambda e, lsl=lsl: e.tensor_tensor(lsl, lsl, psP[:, 512:512 + T4], ALU.add))
                vv_l = P.step("vector", "vv", lambda e, lsl=lsl: e.tensor_tensor(
                    lsl, lsl, psP[:, 768:768 + T4], ALU.add))
                free["psA"] = ("vv", vv_l)
                free["psB"] = ("vv", vv_l)
            P.do("vector", lambda e: e.tensor_tensor(
                sq[:, 0:T4], bcb[:, T4:2 * T4], bcb[:, 0:T4], ALU.subtract))
            for jexp, (scale, op, woff) in enumerate(
                    ((1.0, ALU.is_le, 0), (-1.0, ALU.is_gt, 1))):
                vvd = P.cnt["vv"]
                P.wait("scalar", "vv", vvd)
                svp = P.step("scalar", "sv", lambda e, scale=scale: e.activation(
                    sq[:, 256:256 + T4], sq[:, 0:T4], AF.Exp, scale=scale))
                P.wait("vector", "sv", svp)
                P.do("vector", lambda e: e.tensor_scalar_add(
                    sq[:, 256:256 + T4], sq[:, 256:256 + T4], 1.0))
                P.do("vector", lambda e: e.reciprocal(
                    sq[:, 256:256 + T4], sq[:, 256:256 + T4]))
                P.do("vector", lambda e, op=op: e.tensor_single_scalar(
                    sq[:, 512:512 + T4], sq[:, 0:T4], 0.0, op))
                vvw = P.step("vector", "vv", lambda e, woff=woff: e.tensor_tensor(
                    bcb[:, woff * T4:(woff + 1) * T4], sq[:, 256:256 + T4],
                    sq[:, 512:512 + T4], ALU.mult))
            free["sq_tmp"] = ("vv", vvw)
            P.wait("vector", "gq_g", free["g4"][1])
            vv_g0 = P.step("vector", "vv", lambda e: e.memset(g4[:], 0.0))

            wfc = (d["w_fc0_t"], d["w_fc1_t"])
            wpj = (d["w_pj0_t"], d["w_pj1_t"])
            proj_tv_prev = [0]
            for e_i in range(2):
                for hq in range(16):
                    for q in range(4):
                        mb = hq * 4 + q
                        ring.request(wfc[e_i][2 * mb])
                        ring.request(wfc[e_i][2 * mb + 1])
                    vv_a4 = 0
                    for q in range(4):
                        mb = hq * 4 + q
                        sa, apa = ring.use()
                        sb_, apb = ring.use()
                        psl = psP[:, 0:512] if (q % 2 == 0) else psP[:, 512:1024]
                        pname = "psA" if (q % 2 == 0) else "psB"
                        twait(pname)
                        tvf = mm_group(None, [
                            (lambda e, k=k, psl=psl, apa=apa, apb=apb: e.matmul(
                                psl[:, 0:T4],
                                (apa if k < 8 else apb)[:, (k % 8) * 128:(k % 8 + 1) * 128],
                                xnf[:, k * T4:(k + 1) * T4],
                                start=(k == 0), stop=(k == 15)))
                            for k in range(16)])
                        ring.release(sa, tvf)
                        ring.release(sb_, tvf)
                        P.wait("scalar", "tv", tvf)
                        P.wait("scalar", "vv", free["tmp4"][1])
                        sv_rl = P.step("scalar", "sv", lambda e, psl=psl: e.activation(
                            tmp4[:], psl[:, 0:T4], AF.Relu, scale=0.5))
                        P.wait("vector", "sv", sv_rl)
                        P.wait("vector", "tv", tvf)
                        vv_pre = P.step("vector", "vv", lambda e, psl=psl: e.scalar_tensor_tensor(
                            tmp4[:], psl[:, 0:T4], 0.5, tmp4[:], ALU.mult, ALU.add))
                        free[pname] = ("vv", vv_pre)
                        P.wait("scalar", "vv", vv_pre)
                        svq = P.step("scalar", "sv", lambda e: e.activation(
                            tmp4[:], tmp4[:], AF.Square))
                        P.wait("vector", "sv", svq)
                        P.wait("vector", "vv", vv_g0)
                        P.wait("vector", "tv", proj_tv_prev[0])
                        vv_a4 = P.step("vector", "vv", lambda e, q=q, e_i=e_i: e.tensor_tensor(
                            a4[:, q * T4:(q + 1) * T4], tmp4[:],
                            bcb[:, e_i * T4:(e_i + 1) * T4], ALU.mult))
                        free["tmp4"] = ("vv", vv_a4)
                    for half in range(2):
                        for kq in range(4):
                            ring.request(wpj[e_i][2 * (hq * 4 + kq) + half])
                        tiles = [ring.use() for _ in range(4)]
                        P.wait("tensor", "vv", vv_a4)
                        tvp_last = 0
                        for m8 in range(8):
                            mg = half * 8 + m8
                            psl = psP[:, 0:512] if (mg % 2 == 0) else psP[:, 512:1024]
                            pname = "psA" if (mg % 2 == 0) else "psB"
                            twait(pname)
                            tvp = mm_group(None, [
                                (lambda e, kq=kq, m8=m8, psl=psl, tiles=tiles: e.matmul(
                                    psl[:, 0:T4], tiles[kq][1][:, m8 * 128:(m8 + 1) * 128],
                                    a4[:, kq * T4:(kq + 1) * T4],
                                    start=(kq == 0), stop=(kq == 3)))
                                for kq in range(4)])
                            tvp_last = tvp
                            P.wait("vector", "tv", tvp)
                            vvg = P.step("vector", "vv", lambda e, mg=mg, psl=psl: e.tensor_tensor(
                                g4[:, mg * T4:(mg + 1) * T4], g4[:, mg * T4:(mg + 1) * T4],
                                psl[:, 0:T4], ALU.add))
                            free[pname] = ("vv", vvg)
                        for t in tiles:
                            ring.release(t[0], tvp_last)
                    proj_tv_prev[0] = P.cnt["tv"]
            P.wait("scalar", "vv", P.cnt["vv"])
            svg = P.step("scalar", "sv", lambda e: e.activation(g4[:], g4[:], AF.Silu))
            P.wait("scalar", "sv", svg)
            gqg = dma_feat("scalar", "gq_g", _rearr_feat(d["g_scr"])[:, :, tok0:tok0 + T4],
                           g4[:].rearrange("p (b t) -> p b t", b=KB), KB)
            free["g4"] = ("gq_g", gqg)
            return gqg

        def p5_combine(j, g_need, s_need):
            tok0 = j * T2
            g0 = (j * T2) % 2048
            xc = xs_s[:, 0:KB * T2]
            sc = res_s[:, 0:KB * T2]
            gc = del_s[:, 0:KB * T2]
            P.wait("sync", "gq_g", g_need)
            P.wait("sync", "gq_s", s_need)
            P.wait("sync", "vv", p5_vfree[0])
            P.wait("sync", "gq_out", p5_gfree[0])
            dma_feat("sync", "sq_p5", xc.rearrange("p (b t) -> p b t", b=KB),
                     _rearr_feat(d["xT"])[:, :, tok0:tok0 + T2], KB)
            dma_feat("sync", "sq_p5", sc.rearrange("p (b t) -> p b t", b=KB),
                     _rearr_feat(d["s_scr"])[:, :, tok0:tok0 + T2], KB)
            ldg = dma_feat("sync", "sq_p5", gc.rearrange("p (b t) -> p b t", b=KB),
                           _rearr_feat(d["g_scr"])[:, :, g0:g0 + T2], KB)
            P.wait("vector", "sq_p5", ldg)
            P.wait("vector", "gq_out", p5_gfree[0])
            P.do("vector", lambda e, sc=sc, gc=gc: e.tensor_tensor(sc, sc, gc, ALU.mult))
            vvo = P.step("vector", "vv", lambda e, sc=sc, xc=xc: e.tensor_tensor(
                sc, sc, xc, ALU.add))
            p5_vfree[0] = vvo
            P.wait("scalar", "vv", vvo)
            gqo = dma_feat("scalar", "gq_out", _rearr_feat(d["outT"])[:, :, tok0:tok0 + T2],
                           sc.rearrange("p (b t) -> p b t", b=KB), KB)
            p5_gfree[0] = gqo

        if PHASES == "pre":
            pass
        elif PHASES == "p1ln":
            for ci in range(NC1):
                p1_chunk(ci, ln_only=True)
        elif PHASES.startswith("p1n"):
            for ci in range(int(PHASES[3:])):
                p1_chunk(ci)
        else:
            for ci in range(NC1):
                p1_chunk(ci)
        for fam in ("gq_xn", "gq_xs", "gq_res"):
            gq_p1_done[fam] = P.cnt.get(fam, 0)
        s_done = []
        if PHASES == "pre" or PHASES == "p1ln" or PHASES.startswith("p1n"):
            pass
        elif PHASES in ("p12", "p124", "all"):
            for j in range(NS2):
                s_done.append(p2_super(j))
        g_done = []
        if PHASES in ("p124", "all"):
            for fci in range(NC4):
                g_done.append(p4_chunk(fci))
        if PHASES == "all":
            p5_vfree = [P.cnt["vv"]]
            p5_gfree = [0]
            for j in range(NS2):
                gi = ((j * T2) % 2048) // T4
                p5_combine(j, g_done[gi], s_done[j])
        for fam in ("gq_out", "gq_xn", "gq_xs", "gq_res", "gq_s", "gq_g"):
            tot = P.cnt.get(fam, 0)
            P.do("gpsimd", lambda e, f=fam, t=tot: e.wait_ge(P.sems[f], t))
            P.do("sync", lambda e, f=fam, t=tot: e.wait_ge(P.sems[f], t))

        with nc.Block() as block:
            @block.sync
            def _(eng):
                for f in P.ops["sync"]:
                    f(eng)

            @block.tensor
            def _(eng):
                for f in P.ops["tensor"]:
                    f(eng)

            @block.vector
            def _(eng):
                for f in P.ops["vector"]:
                    f(eng)

            @block.scalar
            def _(eng):
                for f in P.ops["scalar"]:
                    f(eng)

            @block.gpsimd
            def _(eng):
                for f in P.ops["gpsimd"]:
                    f(eng)

    return nc


def _prep_shared(inp):
    f32 = np.float32
    W_in = np.ascontiguousarray(inp["L0_attn_W_in"], dtype=f32)
    W_xp = np.ascontiguousarray(inp["L0_attn_W_xp"], dtype=f32)
    W_dt = np.ascontiguousarray(inp["L0_attn_W_dt"], dtype=f32)
    W_out = np.ascontiguousarray(inp["L0_attn_W_out"], dtype=f32)
    w_fc = np.ascontiguousarray(inp["L0_ffn_w_fc"], dtype=f32)
    w_pj = np.ascontiguousarray(inp["L0_ffn_w_proj"], dtype=f32)
    w_router = np.ascontiguousarray(inp["L0_ffn_w_router"], dtype=f32)

    def ktiles(w):
        Kdim, Mdim = w.shape
        kb, mb = Kdim // 128, Mdim // 128
        t = w.reshape(kb, 128, mb, 128).transpose(2, 1, 0, 3)
        t = t.reshape(mb, 128, kb // 8, 8 * 128).transpose(0, 2, 1, 3)
        return np.ascontiguousarray(t.reshape(mb * (kb // 8), 128, 1024))

    def krows(w):
        Kdim, Mdim = w.shape
        kb = Kdim // 128
        t = w.reshape(kb, 128, Mdim // 1024, 1024).transpose(0, 2, 1, 3)
        return np.ascontiguousarray(t.reshape(kb * (Mdim // 1024), 128, 1024))

    sh = {}
    sh["w_in_t"] = ktiles(W_in)
    sh["w_xp_l"] = np.ascontiguousarray(
        W_xp.reshape(DB, 128, 160).transpose(1, 0, 2).reshape(128, DB * 160))
    sh["w_dt_t"] = np.ascontiguousarray(
        W_dt.reshape(128, 4, 1024).transpose(1, 0, 2))
    sh["w_out_t"] = ktiles(W_out)
    sh["w_fc0_t"] = ktiles(np.ascontiguousarray(w_fc[0].T))
    sh["w_fc1_t"] = ktiles(np.ascontiguousarray(w_fc[1].T))
    sh["w_pj0_t"] = krows(np.ascontiguousarray(w_pj[0].T))
    sh["w_pj1_t"] = krows(np.ascontiguousarray(w_pj[1].T))
    sh["wr_l"] = np.ascontiguousarray(
        w_router.T.reshape(KB, 128, 2).transpose(1, 0, 2).reshape(128, 32))
    sh["wnorm_l"] = np.ascontiguousarray(
        np.asarray(inp["L0_attn_w_norm"], dtype=f32).reshape(KB, 128).T)
    sh["convw_l"] = np.ascontiguousarray(
        np.asarray(inp["L0_attn_conv_w"], dtype=f32).reshape(4, DB, 128)
        .transpose(2, 1, 0).reshape(128, DB * 4))
    sh["convb_l"] = np.ascontiguousarray(
        np.asarray(inp["L0_attn_conv_b"], dtype=f32).reshape(DB, 128).T)
    sh["bdt_l"] = np.ascontiguousarray(
        np.asarray(inp["L0_attn_b_dt"], dtype=f32).reshape(DB, 128).T)
    sh["alog_l"] = np.ascontiguousarray(
        np.asarray(inp["L0_attn_A_log"], dtype=f32).reshape(DB, 128, D_STATE)
        .transpose(1, 0, 2).reshape(128, DB * D_STATE))
    sh["d_l"] = np.ascontiguousarray(
        np.asarray(inp["L0_attn_D"], dtype=f32).reshape(DB, 128).T)
    oh = np.zeros((16, 16 * 128), f32)
    for n in range(16):
        oh[n, n * 128:(n + 1) * 128] = 1.0
    sh["oh16"] = oh
    sh["ones_sq"] = np.ones((128, 128), f32)
    return sh


def kernel(**inputs):
    x = np.ascontiguousarray(np.asarray(inputs["L0_attn_x"], dtype=np.float32))
    B = x.shape[0]
    sh = _prep_shared(inputs)
    nc = build_program()
    in_maps = []
    for c in range(N_CORES):
        b, o = c // 2, c % 2
        m = dict(sh)
        m["xT"] = np.ascontiguousarray(x[b].T)
        m["xoT"] = np.ascontiguousarray(x[b, o * 2048:(o + 1) * 2048].T)
        in_maps.append(m)
    res = run_bass_kernel_spmd(nc, in_maps, core_ids=list(range(N_CORES)))
    out = np.empty((B, L_FULL, D_MODEL), np.float32)
    for c in range(N_CORES):
        b, o = c // 2, c % 2
        outT = np.asarray(res.results[c]["outT"])
        out[b, o * 2048:(o + 1) * 2048, :] = outT[:, o * 2048:(o + 1) * 2048].T
    return out
